# version 01
# baseline (speedup 1.0000x reference)
import contextlib
import numpy as np
import ml_dtypes
import concourse.bass as bass
import concourse.mybir as mybir
from concourse.bass_utils import run_bass_kernel_spmd

F32 = mybir.dt.float32
BF16 = mybir.dt.bfloat16
I32 = mybir.dt.int32
AF = mybir.ActivationFunctionType
ALU = mybir.AluOpType

B, S, H = 2, 8192, 4096
DSSM, NH, PD, ND, KCONV = 8192, 64, 128, 128, 4
CONV_DIM = DSSM + 2 * ND
PROJ_DIM = DSSM + CONV_DIM + NH
EPS = 1e-5
SEG = 2048
NSEG = S // SEG
PRE = (NSEG - 1) * SEG
PREW = 2048
LCH = 128
NCH = SEG // LCH
KC = H // 128
XW = 4 + SEG
XCOL0 = H + DSSM
COL_Z = 0
COL_X = DSSM
COL_B = 2 * DSSM
COL_C = 2 * DSSM + ND
COL_DT = 2 * DSSM + 2 * ND

ENGS = ("sync", "tensor", "vector", "scalar", "gpsimd")


class Sem:
    def __init__(self, handle):
        self.h = handle
        self.n = 0


class Prog:

    def __init__(self):
        self.streams = {e: [] for e in ENGS}

    def raw(self, eng, fn):
        self.streams[eng].append(fn)

    def op(self, eng, fn, mark=None, inc=1):
        if mark is not None:
            mark.n += inc
            def wrap(e, fn=fn, mark=mark, inc=inc):
                fn(e)
                e.drain().then_inc(mark.h, inc)
            self.raw(eng, wrap)
            return mark.n
        self.raw(eng, lambda e, fn=fn: fn(e))
        return None

    def dma(self, q, out, in_, sem, **kw):
        sem.n += 16
        def wrap(e, out=out, in_=in_, sem=sem, kw=kw):
            e.dma_start(out=out, in_=in_, **kw).then_inc(sem.h, 16)
        self.raw(q, wrap)
        return sem.n

    def pe_drain_mark(self, sem, inc=1):
        sem.n += inc
        def wrap(e, sem=sem, inc=inc):
            e.drain().then_inc(sem.h, inc)
        self.raw("tensor", wrap)
        return sem.n

    def wait(self, eng, sem, val=None):
        v = sem.n if val is None else val
        if v <= 0:
            return v
        def wrap(e, sem=sem, v=v):
            e.wait_ge(sem.h, v)
        self.raw(eng, wrap)
        return v

    def barrier(self, by_eng, dma_sems=()):
        targets = {}
        for eng in ENGS:
            s = by_eng[eng]
            s.n += 1
            if eng in ("tensor", "vector", "scalar", "gpsimd"):
                def wrap(e, s=s):
                    e.drain().then_inc(s.h, 1)
            else:
                def wrap(e, s=s):
                    e.nop().then_inc(s.h, 1)
            self.raw(eng, wrap)
            targets[eng] = s.n
        for eng in ENGS:
            for oeng in ENGS:
                if oeng != eng:
                    self.wait(eng, by_eng[oeng], targets[oeng])
            for ds in dma_sems:
                self.wait(eng, ds)

    def emit(self, block):
        st = self.streams
        @block.sync
        def _(e):
            for fn in st["sync"]:
                fn(e)
        @block.tensor
        def _(e):
            for fn in st["tensor"]:
                fn(e)
        @block.vector
        def _(e):
            for fn in st["vector"]:
                fn(e)
        @block.scalar
        def _(e):
            for fn in st["scalar"]:
                fn(e)
        @block.gpsimd
        def _(e):
            for fn in st["gpsimd"]:
                fn(e)


class Ring:

    def __init__(self, bufs):
        self.bufs = bufs
        self.rel = [None] * len(bufs)
        self.i = 0

    def acquire(self, pb, *engs):
        slot = self.i % len(self.bufs)
        if self.rel[slot] is not None:
            sem, val = self.rel[slot]
            for eng in engs:
                pb.wait(eng, sem, val)
        self.i += 1
        return self.bufs[slot]

    def release(self, sem, val):
        slot = (self.i - 1) % len(self.bufs)
        self.rel[slot] = (sem, val)


def host_constants():
    r = np.arange(128)
    c = {}
    c["c_ident"] = np.eye(128).astype(ml_dtypes.bfloat16)
    c["c_tri_strict"] = (r[:, None] > r[None, :]).astype(ml_dtypes.bfloat16)
    c["c_tri_incl"] = (r[:, None] <= r[None, :]).astype(ml_dtypes.bfloat16)
    c["c_mneg"] = np.where(r[None, :] < r[:, None], -30000.0, 0.0).astype(ml_dtypes.bfloat16)
    c["c_ones128"] = np.ones((128, 128), ml_dtypes.bfloat16)
    c["c_ones1"] = np.ones((1, 128), ml_dtypes.bfloat16)
    return c


CONSTS_NP = host_constants()



class Build:
    def __init__(self, cfg):
        self.cfg = cfg
        self.nc = bass.Bass("TRN2", target_bir_lowering=False, detect_race_conditions=False)
        self.pb = Prog()
        self.ps_rel = {}
        self.ctx = contextlib.ExitStack()
        self.dbg = {}
        self._declare_io()
        self._declare_sbuf()

    def _declare_io(self):
        nc, cfg = self.nc, self.cfg
        inp = lambda name, shape, dt=F32: nc.dram_tensor(name, shape, dt, kind="ExternalInput").ap()
        self.xin = inp("xin", [128 + SEG, H])
        self.w_x = inp("w_x", [NH, KC, 128, 128])
        self.w_bc = inp("w_bc", [2, KC, 128, 128])
        self.w_dtt = inp("w_dtt", [KC, 128, 64])
        if not cfg.get("light_io"):
            self.w_z = inp("w_z", [16, KC, 128, 512])
            self.w_out_t = inp("w_out_t", [16, 64, 128, 256])
            self.nw_t = inp("nw_t", [128, 64])
        self.conv_w = inp("conv_w", [128, 66 * KCONV])
        self.dt_bias = inp("dt_bias", [1, NH])
        self.a_log = inp("a_log", [1, NH])
        self.d_par = inp("d_par", [1, NH])
        if not cfg.get("light_io"):
            self.ffn_q = inp("ffn_q", [H, H], I32)
            self.ffn_sc = inp("ffn_sc", [H, 64])
            self.ffn_zp = inp("ffn_zp", [H, 64])
            self.ffn_b = inp("ffn_b", [1, H])
        if cfg.get("pred_inputs"):
            self.s_pred = inp("s_pred", [3, NH, 128, 128])
            self.logd_pred = inp("logd_pred", [1, 3 * NH])
        self.cst = {k: inp(k, list(v.shape), BF16) for k, v in CONSTS_NP.items()}
        self.fc_in = inp("fconsts_in", [128, 8])
        if cfg.get("r_init_input"):
            self.r_init = inp("r_init", [NH, 128, 128])
        if cfg.get("prefix_tokens", 0):
            self.xpre = inp("xpre", [cfg["prefix_tokens"], H])
        self.out_h = nc.dram_tensor("out", [SEG, H], F32, kind="ExternalOutput").ap()
        if cfg.get("state_out"):
            self.s_local_h = nc.dram_tensor("s_local", [NH, 128, 128], F32, kind="ExternalOutput").ap()
            self.log_d_h = nc.dram_tensor("log_d", [1, NH], F32, kind="ExternalOutput").ap()
        self.sz_spill = nc.dram_tensor("sz_spill", [NCH, 128, DSSM], BF16)
        self.y_spill = nc.dram_tensor("y_spill", [NCH, NH, 128, 128], BF16)
        self.xn_spill = nc.dram_tensor("xn_spill", [NCH, KC, 128, 128], BF16)
        self.u_spill = nc.dram_tensor("u_spill", [NH, NCH, 128, 128], BF16)
        self.g_spill = nc.dram_tensor("g_spill", [NCH, 128, H], BF16)
        self.r_spill = nc.dram_tensor("r_spill", [NH, 128, 128], F32)
        if cfg.get("prefix_tokens", 0):
            npre = cfg["prefix_tokens"]
            self.xnpre_spill = nc.dram_tensor("xnpre_spill", [npre // 128, KC, 128, 128], BF16)

    def dbg_out(self, name, shape, dt=F32):
        if name not in self.dbg:
            self.dbg[name] = self.nc.dram_tensor("dbg_" + name, shape, dt, kind="ExternalOutput").ap()
        return self.dbg[name]

    def _declare_sbuf(self):
        nc, E = self.nc, self.ctx.enter_context
        SB = lambda n, sh, dt=F32: E(nc.sbuf_tensor(n, sh, dt))
        PS = lambda n, sh, dt=F32: E(nc.psum_tensor(n, sh, dt))
        self.sems = {}
        for name in ("sW", "sG", "sA", "sPE", "sDV", "sAC", "sSY", "sGP",
                     "sS0", "sS1", "sS2", "sS3", "sQ0", "sQ1", "sT0", "sT1", "sY"):
            self.sems[name] = Sem(E(nc.semaphore(name)))
        s = self.sems
        self.sW, self.sG, self.sA = s["sW"], s["sG"], s["sA"]
        self.sPE, self.sDV, self.sAC, self.sSY = s["sPE"], s["sDV"], s["sAC"], s["sSY"]
        self.sGP = s["sGP"]
        self.sS = [s["sS0"], s["sS1"], s["sS2"], s["sS3"]]
        self.sQ = [s["sQ0"], s["sQ1"]]
        self.sT = [s["sT0"], s["sT1"]]
        self.sY = s["sY"]
        self.all_dma = (self.sW, self.sG, self.sA, s["sS0"], s["sS1"], s["sS2"],
                        s["sS3"], s["sQ0"], s["sQ1"], s["sT0"], s["sT1"], s["sY"])
        self.by_eng = {"sync": self.sSY, "tensor": self.sPE, "vector": self.sDV,
                       "scalar": self.sAC, "gpsimd": self.sGP}

        self.SB = SB
        self.PS = PS
        self.scopes = []
        self.sb_c = {k: SB("sb_" + k, list(CONSTS_NP[k].shape), BF16) for k in CONSTS_NP}
        self.sb_negA = SB("negA", [128, 64])
        self.sb_Dbc = SB("Dbc", [128, 64])
        self.sb_row = SB("row", [1, 64])
        self.sb_dtb = SB("dtbrow", [1, 64])
        self.sb_row2 = SB("row2", [1, 64])
        self.sb_rowbf = SB("rowbf", [1, 128], BF16)
        self.sb_cw = SB("convw", [128, 68 * KCONV])
        self.sb_nw = SB("nwt", [128, 64])
        self.sb_fc = SB("fconsts", [128, 8])
        self.sb_junk = SB("junk", [128, 8])
        self.n_fc = 0
        self.ps_a = PS("ps_a", [128, 512])
        self.ps_b = PS("ps_b", [128, 512])
        self.ps_dm = PS("ps_dm", [128, 128])
        self.ps_y = PS("ps_y", [128, 128])
        self.ps_yi = PS("ps_yi", [128, 128])
        self.ps_s = PS("ps_s", [128, 128])
        self.ps_xt = Ring([PS("ps_xt0", [128, 1024], BF16), PS("ps_xt1", [128, 1024], BF16)])

    def fconst(self, value):
        key = float(value)
        if not hasattr(self, "_fc_map"):
            self._fc_map = {}
        if key not in self._fc_map:
            i = self.n_fc
            assert i < 8
            self.n_fc += 1
            self._fc_map[key] = i
            self.nc.const_aps.register(F32, key, self.sb_fc[:, i:i + 1])
        return key

    def push_scope(self):
        st = contextlib.ExitStack()
        self.scopes.append(st)
        E = st.enter_context
        sb = lambda n, sh, dt=F32: E(self.nc.sbuf_tensor(n, sh, dt))
        return sb

    def pop_scope(self):
        self.scopes.pop().close()

    def alloc_xn(self):
        sb = self.push_scope()
        self.xn = [sb(f"xn{i}", [128, XW], BF16) for i in range(KC)]

    def alloc_phase_a(self):
        sb = self.push_scope()
        if self.cfg.get("dbg_s1"):
            self.sb_s1dbg = sb("s1dbg", [128, 17 * 8])
        self.xa_ring = Ring([sb(f"xa{i}", [128, H]) for i in range(2)])
        self.xtm_ring = Ring([sb(f"xtmb{i}", [128, H], BF16) for i in range(2)])
        self.sb_dump = sb("dump", [128, H])
        self.sb_s1 = sb("lnws", [128, 16])

    def alloc_phase_b(self):
        sb = self.push_scope()
        self.sb_Bfm = sb("Bfm", [128, SEG], BF16)
        self.sb_Cfm = sb("Cfm", [128, SEG], BF16)
        self.sb_Btm = sb("Btm", [128, NCH * 128], BF16)
        self.sb_G = sb("Gt", [128, NCH * 128], BF16)
        self.sb_dt = sb("dts", [128, NCH * 64])
        self.sb_wddt = sb("wddt", [128, NCH * 64])
        self.sb_expc = sb("expc", [128, NCH * 64])
        self.sb_et = sb("ets", [128, NCH * 64])
        self.sb_cacc = sb("cacc", [128, 128])
        self.sb_ctnh = sb("ctnh", [128, 128])
        self.sb_logd = sb("logd", [128, 64])
        self.sb_labf = sb("labf", [128, 64], BF16)
        self.wring = Ring([sb("wring0", [128, KC * 128], BF16)])
        self.xbc_ring = Ring([sb(f"xbc{i}", [128, 4 + SEG], BF16) for i in range(2)])
        self.xh_ring = Ring([sb(f"xh{i}", [128, SEG], BF16) for i in range(2)])
        self.yh_ring = Ring([sb("yh0", [128, SEG], BF16)])
        self.sb_laT = sb("laT", [128, 128], BF16)
        self.sb_W0p = sb("W0p", [128, 128], BF16)
        self.sb_W0 = self.sb_W0p
        self.sb_xtm = sb("xtm", [128, 128], BF16)
        self.sb_dtx = sb("dtx", [128, 128], BF16)
        self.sb_wdtx = sb("wdtx", [128, 128], BF16)
        self.sb_DI = sb("DI", [128, 128], BF16)
        self.sb_y1 = sb("y1", [128, 128])
        self.sb_R = sb("Rst", [128, 128])
        self.sb_R2 = self.sb_R
        self.sb_Rbf = sb("Rbf", [128, 128], BF16)

    FCONST_VALUES = (EPS, 1.0 / H, 1.0, 1.0 / DSSM)

    def emit_init(self):
        pb, c = self.pb, self.sb_c
        pb.dma("sync", self.sb_fc[:, :], self.fc_in, self.sW)

        for fn in (AF.Square, AF.Relu, AF.Exp, AF.Ln, AF.Tanh, AF.Silu, AF.Copy, AF.Sqrt):
            pb.op("scalar", lambda e, fn=fn: e.activation(out=self.sb_junk[:, 0:4], in_=self.sb_fc[:, 0:4], func=fn))
        pb.op("scalar", lambda e: e.drain().then_inc(self.sAC.h, 1))
        self.sAC.n += 1
        for k in CONSTS_NP:
            pb.dma("sync", c[k][:, :], self.cst[k], self.sW)
        pb.dma("sync", self.sb_row[:, :], self.a_log, self.sW)
        pb.dma("sync", self.sb_dtb[:, :], self.dt_bias, self.sW)
        pb.dma("sync", self.sb_row2[:, :], self.d_par, self.sW)
        pb.dma("sync", self.sb_cw[:, 0:66 * KCONV], self.conv_w, self.sW)
        if not self.cfg.get("light_io"):
            pb.dma("sync", self.sb_nw[:, :], self.nw_t, self.sW)
        t = pb.wait("scalar", self.sW)
        pb.op("scalar", lambda e: e.activation(out=self.sb_row[:, :], in_=self.sb_row[:, :], func=AF.Exp))
        pb.op("scalar", lambda e: e.copy(out=self.sb_rowbf[:, 0:64], in_=self.sb_row[:, :]))
        t = pb.op("scalar", lambda e: e.copy(out=self.sb_rowbf[:, 64:128], in_=self.sb_row2[:, :]), mark=self.sAC)
        pb.wait("tensor", self.sAC, t)
        pb.op("tensor", lambda e: e.matmul(out=self.ps_a[:, 0:64], lhsT=self.sb_c["c_ones1"][:, :], rhs=self.sb_rowbf[:, 0:64], start=True, stop=True))
        pb.op("tensor", lambda e: e.matmul(out=self.ps_a[:, 64:128], lhsT=self.sb_c["c_ones1"][:, :], rhs=self.sb_rowbf[:, 64:128], start=True, stop=True))
        t = pb.pe_drain_mark(self.sPE)
        pb.wait("vector", self.sPE, t)
        pb.op("vector", lambda e: e.tensor_scalar(out=self.sb_negA[:, :], in0=self.ps_a[:, 0:64], scalar1=-1.0, scalar2=None, op0=ALU.mult))
        t = pb.op("vector", lambda e: e.tensor_copy(out=self.sb_Dbc[:, :], in_=self.ps_a[:, 64:128]), mark=self.sDV)
        pb.barrier(self.by_eng, dma_sems=self.all_dma)

    def emit_prefix_combine(self):
        pb = self.pb
        sb = self.push_scope()
        ldrow = sb("ldrow", [1, 3 * NH])
        frow = sb("frow", [1, 2 * NH])
        frowbf = sb("frowbf", [1, 2 * NH], BF16)
        fbc = sb("fbc", [128, 2 * NH])
        spr = Ring([sb(f"spcmb{i}", [128, 3 * 128]) for i in range(2)])
        rstr = Ring([sb(f"rcmb{i}", [128, 128]) for i in range(2)])
        t_ld = pb.dma("sync", ldrow[:, :], self.logd_pred, self.sW)
        pb.wait("vector", self.sW, t_ld)
        t = pb.op("vector", lambda e: e.tensor_tensor(
            out=frow[:, 0:NH], in0=ldrow[:, NH:2 * NH], in1=ldrow[:, 2 * NH:3 * NH],
            op=ALU.add), mark=self.sDV)
        pb.wait("vector", self.sDV, t)
        t = pb.op("vector", lambda e: e.tensor_copy(
            out=frow[:, NH:2 * NH], in_=ldrow[:, 2 * NH:3 * NH]), mark=self.sDV)
        pb.wait("scalar", self.sDV, t)
        t = pb.op("scalar", lambda e: e.activation(
            out=frow[:, :], in_=frow[:, :], func=AF.Exp), mark=self.sAC)
        pb.wait("scalar", self.sAC, t)
        t = pb.op("scalar", lambda e: e.copy(out=frowbf[:, :], in_=frow[:, :]), mark=self.sAC)
        bank = self.ps_a
        rel = self.ps_rel.get(id(bank))
        if rel is not None:
            pb.wait("tensor", *rel)
        pb.wait("tensor", self.sAC, t)
        pb.op("tensor", lambda e: e.matmul(
            out=bank[:, 0:2 * NH], lhsT=self.sb_c["c_ones1"][:, :], rhs=frowbf[:, :],
            start=True, stop=True))
        t_f = pb.pe_drain_mark(self.sPE)
        pb.wait("vector", self.sPE, t_f)
        t_fb = pb.op("vector", lambda e: e.tensor_copy(out=fbc[:, :], in_=bank[:, 0:2 * NH]),
                     mark=self.sDV)
        self.ps_rel[id(bank)] = (self.sDV, t_fb)
        for h in range(NH):
            sp = spr.acquire(pb, "sync")
            sls = self.sS[(spr.i - 1) % 2]
            t_s = pb.dma("sync", sp.ap().rearrange("n (i p) -> n i p", p=128),
                         self.s_pred[:, h].rearrange("i n p -> n i p"), sls)
            rst = rstr.acquire(pb, "vector")
            pb.wait("vector", sls, t_s)
            if h == 0:
                pb.wait("vector", self.sDV, t_fb)
            t1 = pb.op("vector", lambda e, h=h, sp=sp, rst=rst: e.scalar_tensor_tensor(
                out=rst[:, :], in0=sp[:, 128:256], scalar=fbc[:, NH + h:NH + h + 1],
                in1=sp[:, 256:384], op0=ALU.mult, op1=ALU.add), mark=self.sDV)
            pb.wait("vector", self.sDV, t1)
            t2 = pb.op("vector", lambda e, h=h, sp=sp, rst=rst: e.scalar_tensor_tensor(
                out=rst[:, :], in0=sp[:, 0:128], scalar=fbc[:, h:h + 1],
                in1=rst[:, :], op0=ALU.mult, op1=ALU.add), mark=self.sDV)
            spr.release(self.sDV, t2)
            pb.wait("gpsimd", self.sDV, t2)
            sts = self.sT[(rstr.i - 1) % 2]
            t_sp = pb.dma("gpsimd", self.r_spill[h, :, :], rst[:, :], sts)
            rstr.release(sts, t_sp)
        pb.barrier(self.by_eng, dma_sems=self.all_dma)
        self.pop_scope()

    def emit_ln_stream(self, src_ap, n_tiles, first_rows, write_xn_cols, spill=None):
        pb = self.pb
        row0 = 0
        for i in range(n_tiles):
            q = 8 * (i % 2)
            nrows = 128
            buf = self.xa_ring.acquire(pb, "sync")
            sld = self.sS[i % 2]
            t_load = pb.dma("sync", buf[0:nrows, :], src_ap[row0:row0 + nrows, :], sld)
            pb.wait("vector", sld, t_load)
            pb.op("vector", lambda e, q=q, b=buf, n=nrows: e.tensor_reduce(
                out=self.sb_s1[0:n, q + 0:q + 1], in_=b[0:n, :], axis=mybir.AxisListType.X, op=ALU.add))
            pb.wait("scalar", sld, t_load)
            t_sq = pb.op("scalar", lambda e, q=q, b=buf, n=nrows: e.activation(
                out=self.sb_dump[0:n, :], in_=b[0:n, :], func=AF.Square, accum_out=self.sb_s1[0:n, q + 1:q + 2]), mark=self.sAC)
            pb.wait("vector", self.sAC, t_sq)
            t = pb.op("vector", lambda e, q=q, n=nrows: e.tensor_scalar(
                out=self.sb_s1[0:n, q + 2:q + 3], in0=self.sb_s1[0:n, q + 0:q + 1], scalar1=1.0 / H,
                scalar2=self.sb_s1[0:n, q + 0:q + 1], op0=ALU.mult, op1=ALU.mult), mark=self.sDV)
            pb.wait("vector", self.sDV, t)
            t = pb.op("vector", lambda e, q=q, n=nrows: e.tensor_tensor(
                out=self.sb_s1[0:n, q + 3:q + 4], in0=self.sb_s1[0:n, q + 1:q + 2], in1=self.sb_s1[0:n, q + 2:q + 3], op=ALU.subtract), mark=self.sDV)
            pb.wait("vector", self.sDV, t)
            t = pb.op("vector", lambda e, q=q, n=nrows: e.tensor_scalar(
                out=self.sb_s1[0:n, q + 7:q + 8], in0=self.sb_s1[0:n, q + 3:q + 4],
                scalar1=0.0, scalar2=None, op0=ALU.max), mark=self.sDV)
            pb.wait("vector", self.sDV, t)
            t = pb.op("vector", lambda e, q=q, n=nrows: e.tensor_scalar(
                out=self.sb_s1[0:n, q + 3:q + 4], in0=self.sb_s1[0:n, q + 7:q + 8],
                scalar1=self.sb_fc[0:n, 1:2], scalar2=self.sb_fc[0:n, 0:1], op0=ALU.mult, op1=ALU.add), mark=self.sDV)
            pb.wait("vector", self.sDV, t)
            t_v = pb.op("vector", lambda e, q=q, n=nrows: e.tensor_scalar(
                out=self.sb_s1[0:n, q + 4:q + 5], in0=self.sb_s1[0:n, q + 0:q + 1], scalar1=1.0 / H, scalar2=None, op0=ALU.mult), mark=self.sDV)
            pb.wait("scalar", self.sDV, t_v)
            t = pb.op("scalar", lambda e, q=q, n=nrows: e.activation(
                out=self.sb_s1[0:n, q + 5:q + 6], in_=self.sb_s1[0:n, q + 3:q + 4], func=AF.Ln), mark=self.sAC)
            pb.wait("scalar", self.sAC, t)
            t_rs = pb.op("scalar", lambda e, q=q, n=nrows: e.activation(
                out=self.sb_s1[0:n, q + 6:q + 7], in_=self.sb_s1[0:n, q + 5:q + 6], func=AF.Exp, scale=-0.5), mark=self.sAC)
            if self.cfg.get("dbg_s1"):
                pb.wait("vector", self.sAC, t_rs)
                pb.op("vector", lambda e, q=q, i=i: e.tensor_copy(
                    out=self.sb_s1dbg[:, i * 8:(i + 1) * 8], in_=self.sb_s1[:, q:q + 8]))
            xtmb = self.xtm_ring.acquire(pb, "vector")
            pb.wait("vector", self.sAC, t_rs)
            t_xn = pb.op("vector", lambda e, q=q, b=buf, xb=xtmb, n=nrows: e.tensor_scalar(
                out=xb[0:n, :], in0=b[0:n, :], scalar1=self.sb_s1[0:n, q + 4:q + 5],
                scalar2=self.sb_s1[0:n, q + 6:q + 7], op0=ALU.subtract, op1=ALU.mult), mark=self.sDV)
            self.xa_ring.release(self.sDV, t_xn)
            if self.cfg.get("a_parts") == "ln":
                row0 += nrows
                self.xtm_ring.release(self.sDV, t_xn)
                continue
            pb.wait("tensor", self.sDV, t_xn)
            last_ev = None
            for gi in range(4):
                bank = self.ps_xt.acquire(pb, "tensor")
                for j in range(8):
                    kc = gi * 8 + j
                    pb.op("tensor", lambda e, xb=xtmb, kc=kc, bank=bank, j=j: e.transpose(
                        out=bank[:, j * 128:(j + 1) * 128], in_=xb[:, kc * 128:(kc + 1) * 128],
                        identity=self.sb_c["c_ident"][:, :]))
                t_tp = pb.pe_drain_mark(self.sPE)
                if self.cfg.get("a_parts") == "noevac":
                    self.ps_xt.release(self.sPE, t_tp)
                    last_ev = t_tp
                    continue
                pb.wait("scalar", self.sPE, t_tp)
                t_ev = None
                dst_col, ncols, soff = write_xn_cols[i]
                for j in range(8):
                    kc = gi * 8 + j
                    mk = self.sAC if j == 7 else None
                    t_ev = pb.op("scalar", lambda e, kc=kc, bank=bank, j=j, n=ncols, o=soff, dc=dst_col: e.copy(
                        out=self.xn[kc][:, dc:dc + n], in_=bank[:, j * 128 + o:j * 128 + o + n]), mark=mk)
                self.ps_xt.release(self.sAC, t_ev)
                last_ev = t_ev
            self.xtm_ring.release(self.sPE, self.sPE.n)
            if self.cfg.get("a_serial"):
                pb.barrier(self.by_eng, dma_sems=self.all_dma)
            if spill is not None:
                dram, toff = spill
                pb.wait("gpsimd", self.sAC, last_ev)
                for kc in range(KC):
                    dst_col = write_xn_cols[i][0]
                    pb.dma("gpsimd", dram[toff + i, kc, :, 0:nrows],
                           self.xn[kc][:, dst_col:dst_col + nrows], self.sG)
            row0 += nrows

    def emit_phase_a(self):
        self.alloc_phase_a()
        cols = [(0, 4, 124)] + [(4 + t * 128, 128, 0) for t in range(NCH)]
        self.emit_ln_stream(self.xin, len(cols), 128, cols)
        pb = self.pb
        pb.barrier(self.by_eng, dma_sems=self.all_dma)
        self.pop_scope()


    def emit_cc_proj(self, wbuf, dst_bf16, wready):
        pb = self.pb
        streams = [(0, 512), (512, 1024), (1024, 1536), (1536, 2048), (2048, XW)]
        banks = [self.ps_a, self.ps_b]
        t_done = None
        for si, (c0, c1) in enumerate(streams):
            bank = banks[si % 2]
            rel = self.ps_rel.get(id(bank))
            if rel is not None:
                pb.wait("tensor", *rel)
            if si == 0 and wready is not None:
                pb.wait("tensor", wready[0], wready[1])
            for kc in range(KC):
                pb.op("tensor", lambda e, kc=kc, c0=c0, c1=c1, bank=bank, wbuf=wbuf: e.matmul(
                    out=bank[:, 0:c1 - c0], lhsT=wbuf[:, kc * 128:(kc + 1) * 128],
                    rhs=self.xn[kc][:, c0:c1], start=(kc == 0), stop=(kc == KC - 1)))
            t_mm = pb.pe_drain_mark(self.sPE)
            pb.wait("scalar", self.sPE, t_mm)
            t_ev = pb.op("scalar", lambda e, c0=c0, c1=c1, bank=bank, dst=dst_bf16: e.copy(
                out=dst[:, c0:c1], in_=bank[:, 0:c1 - c0]), mark=self.sAC)
            self.ps_rel[id(bank)] = (self.sAC, t_ev)
            t_done = t_ev
        return t_done

    def emit_conv_silu(self, xbc, cchunk, dst, n_out=SEG, use_true_silu=False, dst_col0=0):
        pb = self.pb
        HALF = 128
        for h0 in range(0, n_out, HALF):
            hw = min(HALF, n_out - h0)
            acc, tnh = self.sb_cacc, self.sb_ctnh
            t = pb.op("vector", lambda e, h0=h0, hw=hw, acc=acc, xbc=xbc, cc=cchunk: e.tensor_scalar(
                out=acc[:, 0:hw], in0=xbc[:, 1 + h0:1 + h0 + hw],
                scalar1=self.sb_cw[:, cc * KCONV:cc * KCONV + 1], scalar2=None, op0=ALU.mult), mark=self.sDV)
            t_acc = None
            for k in range(1, KCONV):
                pb.wait("vector", self.sDV, t)
                t = t_acc = pb.op("vector", lambda e, h0=h0, hw=hw, k=k, acc=acc, xbc=xbc, cc=cchunk: e.scalar_tensor_tensor(
                    out=acc[:, 0:hw], in0=xbc[:, 1 + h0 + k:1 + h0 + k + hw],
                    scalar=self.sb_cw[:, cc * KCONV + k:cc * KCONV + k + 1],
                    in1=acc[:, 0:hw], op0=ALU.mult, op1=ALU.add), mark=self.sDV)
            pb.wait("scalar", self.sDV, t_acc)
            if use_true_silu:
                t_s = pb.op("scalar", lambda e, h0=h0, hw=hw, acc=acc, dst=dst, d0=dst_col0: e.activation(
                    out=dst[:, d0 + h0:d0 + h0 + hw], in_=acc[:, 0:hw], func=AF.Silu), mark=self.sAC)
                pb.wait("vector", self.sAC, t_s)
            else:
                t_t = pb.op("scalar", lambda e, hw=hw, acc=acc, tnh=tnh: e.activation(
                    out=tnh[:, 0:hw], in_=acc[:, 0:hw], func=AF.Tanh, scale=0.5), mark=self.sAC)
                pb.wait("vector", self.sAC, t_t)
                t = pb.op("vector", lambda e, hw=hw, tnh=tnh: e.tensor_scalar(
                    out=tnh[:, 0:hw], in0=tnh[:, 0:hw], scalar1=0.5, scalar2=0.5,
                    op0=ALU.mult, op1=ALU.add), mark=self.sDV)
                pb.wait("vector", self.sDV, t)
                last = pb.op("vector", lambda e, h0=h0, hw=hw, acc=acc, tnh=tnh, dst=dst, d0=dst_col0: e.tensor_tensor(
                    out=dst[:, d0 + h0:d0 + h0 + hw], in0=tnh[:, 0:hw], in1=acc[:, 0:hw], op=ALU.mult),
                    mark=self.sDV if h0 + HALF >= n_out else None)
        return self.sDV.n

    def emit_b0(self):
        pb = self.pb
        sb0 = self.push_scope()
        self.sb_sp = [sb0(f"spws{i}", [128, 64]) for i in range(3)]
        self.sb_wdt = sb0("wdtw", [128, KC * 64], BF16)
        self.sb_la = sb0("lasc", [128, 64])
        self.sb_dtbbf = sb0("dtbbf", [1, 64], BF16)
        t_w = pb.dma("gpsimd", self.sb_wdt.ap().rearrange("p (c m) -> p c m", m=64),
                     self.w_dtt.rearrange("c p m -> p c m"), self.sG)
        pb.wait("scalar", self.sW)
        t_bb = pb.op("scalar", lambda e: e.copy(out=self.sb_dtbbf[:, :], in_=self.sb_dtb[:, :]), mark=self.sAC)
        sp0, sp1, sp2 = self.sb_sp
        for tt in range(NCH):
            bank = self.ps_a
            rel = self.ps_rel.get(id(bank))
            if rel is not None:
                pb.wait("tensor", *rel)
            if tt == 0:
                pb.wait("tensor", self.sG, t_w)
                pb.wait("tensor", self.sAC, t_bb)
            for kc in range(KC):
                pb.op("tensor", lambda e, kc=kc, tt=tt, bank=bank: e.matmul(
                    out=bank[:, 0:64], lhsT=self.xn[kc][:, 4 + tt * 128:4 + (tt + 1) * 128],
                    rhs=self.sb_wdt[:, kc * 64:(kc + 1) * 64], start=(kc == 0), stop=False))
            pb.op("tensor", lambda e, bank=bank: e.matmul(
                out=bank[:, 0:64], lhsT=self.sb_c["c_ones1"][:, :], rhs=self.sb_dtbbf[:, :],
                start=False, stop=True))
            t_mm = pb.pe_drain_mark(self.sPE)
            pb.wait("scalar", self.sPE, t_mm)
            t_r = pb.op("scalar", lambda e, bank=bank, sp0=sp0: e.activation(
                out=sp0[:, :], in_=bank[:, 0:64], func=AF.Relu), mark=self.sAC)
            pb.wait("vector", self.sAC, t_r)
            t_v = pb.op("vector", lambda e, bank=bank, sp0=sp0, sp1=sp1: e.scalar_tensor_tensor(
                out=sp1[:, :], in0=sp0[:, :], scalar=-2.0, in1=bank[:, 0:64],
                op0=ALU.mult, op1=ALU.add), mark=self.sDV)
            self.ps_rel[id(bank)] = (self.sDV, t_v)
            pb.wait("scalar", self.sDV, t_v)
            t_e2 = pb.op("scalar", lambda e, sp1=sp1: e.activation(out=sp1[:, :], in_=sp1[:, :], func=AF.Exp), mark=self.sAC)
            pb.wait("vector", self.sAC, t_e2)
            t_p1 = pb.op("vector", lambda e, sp1=sp1: e.tensor_scalar(
                out=sp1[:, :], in0=sp1[:, :], scalar1=1.0, scalar2=None, op0=ALU.add), mark=self.sDV)
            pb.wait("scalar", self.sDV, t_p1)
            t_l = pb.op("scalar", lambda e, sp1=sp1: e.activation(
                out=sp1[:, :], in_=sp1[:, :], func=AF.Ln), mark=self.sAC)
            pb.wait("vector", self.sAC, t_l)
            pb.op("vector", lambda e, tt=tt, sp0=sp0, sp1=sp1: e.tensor_tensor(
                out=self.sb_dt[:, tt * 64:(tt + 1) * 64], in0=sp0[:, :], in1=sp1[:, :], op=ALU.add))
            pb.op("vector", lambda e, tt=tt: e.tensor_tensor(
                out=self.sb_la[:, :], in0=self.sb_dt[:, tt * 64:(tt + 1) * 64],
                in1=self.sb_negA[:, :], op=ALU.mult))
            t_la = pb.op("vector", lambda e: e.tensor_copy(
                out=self.sb_labf[:, :], in_=self.sb_la[:, :]), mark=self.sDV)
            bank2 = self.ps_b
            rel2 = self.ps_rel.get(id(bank2))
            if rel2 is not None:
                pb.wait("tensor", *rel2)
            pb.wait("tensor", self.sDV, t_la)
            pb.op("tensor", lambda e, bank2=bank2: e.matmul(
                out=bank2[:, 0:64], lhsT=self.sb_c["c_tri_incl"][:, :], rhs=self.sb_labf[:, :], start=True, stop=True))
            pb.op("tensor", lambda e, bank2=bank2: e.matmul(
                out=bank2[:, 64:128], lhsT=self.sb_c["c_tri_strict"][:, :], rhs=self.sb_labf[:, :], start=True, stop=True))
            pb.op("tensor", lambda e, bank2=bank2: e.matmul(
                out=bank2[:, 128:192], lhsT=self.sb_c["c_ones128"][:, :], rhs=self.sb_labf[:, :], start=True, stop=True))
            t_dm = pb.pe_drain_mark(self.sPE)
            pb.wait("scalar", self.sPE, t_dm)
            pb.op("scalar", lambda e, tt=tt, bank2=bank2: e.activation(
                out=self.sb_expc[:, tt * 64:(tt + 1) * 64], in_=bank2[:, 0:64], func=AF.Exp))
            pb.op("scalar", lambda e, bank2=bank2, sp2=sp2: e.activation(
                out=sp2[:, :], in_=bank2[:, 64:128], func=AF.Exp))
            t_e = pb.op("scalar", lambda e, tt=tt, bank2=bank2: e.activation(
                out=self.sb_et[:, tt * 64:(tt + 1) * 64], in_=bank2[:, 128:192], func=AF.Exp), mark=self.sAC)
            if self.cfg.get("state_out"):
                pb.wait("vector", self.sAC, t_e)
                t_ls = pb.op("vector", lambda e, bank2=bank2: e.tensor_tensor(
                    out=self.sb_logd[:, :], in0=self.sb_logd[:, :], in1=bank2[:, 128:192], op=ALU.add), mark=self.sDV)
                self.ps_rel[id(bank2)] = (self.sDV, t_ls)
            else:
                self.ps_rel[id(bank2)] = (self.sAC, t_e)
            pb.wait("vector", self.sAC, t_e)
            pb.op("vector", lambda e, tt=tt, sp2=sp2: e.tensor_tensor(
                out=self.sb_wddt[:, tt * 64:(tt + 1) * 64], in0=sp2[:, :],
                in1=self.sb_dt[:, tt * 64:(tt + 1) * 64], op=ALU.mult))

    def emit_b1(self):
        pb = self.pb
        pb.barrier(self.by_eng, dma_sems=self.all_dma)
        self.pop_scope()
        for i, dst in ((0, self.sb_Bfm), (1, self.sb_Cfm)):
            wb = self.wring.acquire(pb, "gpsimd")
            t_w = pb.dma("gpsimd", wb.ap().rearrange("p (c m) -> p c m", m=128),
                         self.w_bc[i].rearrange("c p m -> p c m"), self.sG)
            xb = self.xbc_ring.acquire(pb, "tensor", "scalar")
            t_pj = self.emit_cc_proj(wb, xb, (self.sG, t_w))
            self.wring.release(self.sPE, self.sPE.n)
            if "bcraw" in self.cfg.get("debug", ()) and i == 0:
                od = self.dbg_out("bcraw", [128, XW], F32)
                pb.wait("gpsimd", self.sAC, t_pj)
                pb.dma("gpsimd", od, xb[:, :], self.sG)
                od2 = self.dbg_out("wraw", [128, KC * 128], F32)
                pb.dma("gpsimd", od2, wb[:, :], self.sG)
            pb.wait("vector", self.sAC, t_pj)
            t_cv = self.emit_conv_silu(xb, 64 + i, dst, use_true_silu=True)
            self.xbc_ring.release(self.sDV, t_cv)
        for gi in range(2):
            bank = self.ps_xt.acquire(pb, "tensor")
            pb.wait("tensor", self.sDV)
            for j in range(8):
                k = gi * 8 + j
                pb.op("tensor", lambda e, k=k, j=j, bank=bank: e.transpose(
                    out=bank[:, j * 128:(j + 1) * 128], in_=self.sb_Bfm[:, k * 128:(k + 1) * 128],
                    identity=self.sb_c["c_ident"][:, :]))
            t_tp = pb.pe_drain_mark(self.sPE)
            pb.wait("scalar", self.sPE, t_tp)
            t_ev = None
            for j in range(8):
                k = gi * 8 + j
                t_ev = pb.op("scalar", lambda e, k=k, j=j, bank=bank: e.copy(
                    out=self.sb_Btm[:, k * 128:(k + 1) * 128], in_=bank[:, j * 128:(j + 1) * 128]),
                    mark=self.sAC if j == 7 else None)
            self.ps_xt.release(self.sAC, t_ev)
        for g4 in range(4):
            bank = [self.ps_a, self.ps_b][g4 % 2]
            rel = self.ps_rel.get(id(bank))
            if rel is not None:
                pb.wait("tensor", *rel)
            for j in range(4):
                k = g4 * 4 + j
                pb.op("tensor", lambda e, k=k, j=j, bank=bank: e.matmul(
                    out=bank[:, j * 128:(j + 1) * 128], lhsT=self.sb_Bfm[:, k * 128:(k + 1) * 128],
                    rhs=self.sb_Cfm[:, k * 128:(k + 1) * 128], start=True, stop=True))
            t_g = pb.pe_drain_mark(self.sPE)
            pb.wait("scalar", self.sPE, t_g)
            t_ev = pb.op("scalar", lambda e, g4=g4, bank=bank: e.copy(
                out=self.sb_G[:, g4 * 512:(g4 + 1) * 512], in_=bank[:, 0:512]), mark=self.sAC)
            self.ps_rel[id(bank)] = (self.sAC, t_ev)

    def emit_b2(self):
        pb = self.pb
        cfg = self.cfg
        nheads = cfg.get("b2_heads", NH)
        skip = cfg.get("b2_skip", set())
        for h in range(nheads):
            wb = self.wring.acquire(pb, "gpsimd")
            t_w = pb.dma("gpsimd", wb.ap().rearrange("p (c m) -> p c m", m=128),
                         self.w_x[h].rearrange("c p m -> p c m"), self.sG)
            xb = self.xbc_ring.acquire(pb, "tensor", "scalar")
            t_pj = self.emit_cc_proj(wb, xb, (self.sG, t_w))
            self.wring.release(self.sPE, self.sPE.n)
            xh = self.xh_ring.acquire(pb, "vector")
            pb.wait("vector", self.sAC, t_pj)
            t_cv = self.emit_conv_silu(xb, h, xh, use_true_silu=False)
            self.xbc_ring.release(self.sDV, t_cv)
            if "ssd" in skip:
                self.xh_ring.release(self.sDV, self.sDV.n)
                continue
            pb.op("vector", lambda e, h=h: e.tensor_scalar(
                out=self.sb_DI[:, :], in0=self.sb_c["c_ident"][:, :],
                scalar1=self.sb_Dbc[:, h:h + 1], scalar2=None, op0=ALU.mult))
            yh = self.yh_ring.acquire(pb, "vector", "tensor")
            pb.wait("sync", self.sDV, self.sDV.n)
            if cfg.get("r_init_input"):
                t_r = pb.dma("sync", self.sb_R[:, :], self.r_init[h, :, :], self.sW)
            else:
                t_r = pb.dma("sync", self.sb_R[:, :], self.r_spill[h, :, :], self.sW)
            pb.wait("scalar", self.sW, t_r)
            t_rb = pb.op("scalar", lambda e: e.copy(out=self.sb_Rbf[:, :], in_=self.sb_R[:, :]), mark=self.sAC)
            pb.wait("tensor", self.sAC, t_rb)
            Rcur = Rnxt = self.sb_R
            for k in range(NCH):
                co = k * 64 + h
                t_lt = pb.op("vector", lambda e, co=co, h=h: e.tensor_scalar(
                    out=self.sb_laT[:, :], in0=self.sb_c["c_tri_strict"][:, :],
                    scalar1=self.sb_dt[:, co:co + 1], scalar2=self.sb_negA[:, h:h + 1],
                    op0=ALU.mult, op1=ALU.mult), mark=self.sDV)
                rel = self.ps_rel.get(id(self.ps_dm))
                if rel is not None:
                    pb.wait("tensor", *rel)
                pb.wait("tensor", self.sDV, t_lt)
                pb.op("tensor", lambda e: e.matmul(
                    out=self.ps_dm[:, :], lhsT=self.sb_laT[:, :], rhs=self.sb_c["c_tri_incl"][:, :],
                    start=True, stop=False))
                pb.op("tensor", lambda e: e.matmul(
                    out=self.ps_dm[:, :], lhsT=self.sb_c["c_ident"][:, :], rhs=self.sb_c["c_mneg"][:, :],
                    start=False, stop=True))
                bank_x = self.ps_xt.acquire(pb, "tensor")
                pb.op("tensor", lambda e, k=k, xh=xh, bank_x=bank_x: e.transpose(
                    out=bank_x[:, 0:128], in_=xh[:, k * 128:(k + 1) * 128],
                    identity=self.sb_c["c_ident"][:, :]))
                t_d1 = pb.pe_drain_mark(self.sPE)
                pb.wait("scalar", self.sPE, t_d1)
                t_w0p = pb.op("scalar", lambda e: e.activation(
                    out=self.sb_W0p[:, :], in_=self.ps_dm[:, :], func=AF.Exp), mark=self.sAC)
                self.ps_rel[id(self.ps_dm)] = (self.sAC, t_w0p)
                t_xtm = pb.op("scalar", lambda e, bank_x=bank_x: e.copy(
                    out=self.sb_xtm[:, :], in_=bank_x[:, 0:128]), mark=self.sAC)
                self.ps_xt.release(self.sAC, t_xtm)
                pb.wait("vector", self.sAC, t_xtm)
                pb.op("vector", lambda e, k=k: e.tensor_tensor(
                    out=self.sb_W0[:, :], in0=self.sb_W0p[:, :], in1=self.sb_G[:, k * 128:(k + 1) * 128],
                    op=ALU.mult))
                pb.op("vector", lambda e, co=co: e.tensor_scalar(
                    out=self.sb_dtx[:, :], in0=self.sb_xtm[:, :],
                    scalar1=self.sb_dt[:, co:co + 1], scalar2=None, op0=ALU.mult))
                t_opnd = pb.op("vector", lambda e, co=co: e.tensor_scalar(
                    out=self.sb_wdtx[:, :], in0=self.sb_xtm[:, :],
                    scalar1=self.sb_wddt[:, co:co + 1], scalar2=None, op0=ALU.mult), mark=self.sDV)
                for ps in (self.ps_y, self.ps_yi, self.ps_s):
                    rel = self.ps_rel.get(id(ps))
                    if rel is not None:
                        pb.wait("tensor", *rel)
                pb.wait("tensor", self.sDV, t_opnd)
                pb.op("tensor", lambda e: e.matmul(
                    out=self.ps_y[:, :], lhsT=self.sb_W0[:, :], rhs=self.sb_dtx[:, :], start=True, stop=False))
                pb.op("tensor", lambda e: e.matmul(
                    out=self.ps_y[:, :], lhsT=self.sb_DI[:, :], rhs=self.sb_xtm[:, :], start=False, stop=True))
                pb.op("tensor", lambda e, k=k: e.matmul(
                    out=self.ps_yi[:, :], lhsT=self.sb_Cfm[:, k * 128:(k + 1) * 128],
                    rhs=self.sb_Rbf[:, :], start=True, stop=True))
                pb.op("tensor", lambda e, k=k: e.matmul(
                    out=self.ps_s[:, :], lhsT=self.sb_Btm[:, k * 128:(k + 1) * 128],
                    rhs=self.sb_wdtx[:, :], start=True, stop=True))
                t_d2 = pb.pe_drain_mark(self.sPE)
                pb.wait("scalar", self.sPE, t_d2)
                t_y1 = pb.op("scalar", lambda e, co=co: e.activation(
                    out=self.sb_y1[:, :], in_=self.ps_yi[:, :], func=AF.Copy,
                    scale=self.sb_expc[:, co:co + 1]), mark=self.sAC)
                self.ps_rel[id(self.ps_yi)] = (self.sAC, t_y1)
                pb.wait("vector", self.sAC, t_y1)
                t_ya = pb.op("vector", lambda e, k=k, yh=yh: e.tensor_tensor(
                    out=yh[:, k * 128:(k + 1) * 128], in0=self.ps_y[:, :], in1=self.sb_y1[:, :],
                    op=ALU.add), mark=self.sDV)
                self.ps_rel[id(self.ps_y)] = (self.sDV, t_ya)
                t_ru = pb.op("vector", lambda e, co=co, Rc=Rcur, Rn=Rnxt: e.scalar_tensor_tensor(
                    out=Rn[:, :], in0=Rc[:, :], scalar=self.sb_et[:, co:co + 1],
                    in1=self.ps_s[:, :], op0=ALU.mult, op1=ALU.add), mark=self.sDV)
                self.ps_rel[id(self.ps_s)] = (self.sDV, t_ru)
                if k < NCH - 1:
                    pb.wait("scalar", self.sDV, t_ru)
                    t_rb = pb.op("scalar", lambda e, Rn=Rnxt: e.copy(
                        out=self.sb_Rbf[:, :], in_=Rn[:, :]), mark=self.sAC)
                    pb.wait("tensor", self.sAC, t_rb)
            if "spill" in skip:
                self.yh_ring.release(self.sDV, self.sDV.n)
            else:
                pb.wait("gpsimd", self.sDV)
                t_ydma = pb.dma("gpsimd", self.y_spill.ap().rearrange("t h p c -> h p t c")[h],
                                yh.ap().rearrange("p (t c) -> p t c", c=128), self.sY)
                self.yh_ring.release(self.sY, t_ydma)
            self.xh_ring.release(self.sPE, self.sPE.n)
            if cfg.get("state_out"):
                pb.wait("gpsimd", self.sDV)
                pb.dma("gpsimd", self.s_local_h[h, :, :], Rcur[:, :], self.sG)
        if cfg.get("state_out"):
            pb.wait("gpsimd", self.sDV)
            pb.dma("gpsimd", self.log_d_h, self.sb_logd[0:1, :], self.sG)
        pb.barrier(self.by_eng, dma_sems=self.all_dma)


    def emit_b3(self):
        pb = self.pb
        sb = self.push_scope()
        zring = Ring([sb(f"zslab{i}", [128, KC * 256], BF16) for i in range(2)])
        zstg = Ring([sb(f"zstg{i}", [128, 256], BF16) for i in range(2)])
        banks = [self.ps_a, self.ps_b]
        for g in range(32):
            wz = zring.acquire(pb, "tensor")
            t_w = pb.dma("gpsimd", wz.ap().rearrange("p (c m) -> p c m", m=256),
                         self.w_z[g // 2, :, :, (g % 2) * 256:(g % 2 + 1) * 256].rearrange("c p m -> p c m"),
                         self.sG)
            for tt in range(NCH):
                bank = banks[(g * NCH + tt) % 2]
                rel = self.ps_rel.get(id(bank))
                if rel is not None:
                    pb.wait("tensor", *rel)
                if tt == 0:
                    pb.wait("tensor", self.sG, t_w)
                for kc in range(KC):
                    pb.op("tensor", lambda e, kc=kc, tt=tt, bank=bank, wz=wz: e.matmul(
                        out=bank[:, 0:256], lhsT=self.xn[kc][:, 4 + tt * 128:4 + (tt + 1) * 128],
                        rhs=wz[:, kc * 256:(kc + 1) * 256], start=(kc == 0), stop=(kc == KC - 1)))
                t_mm = pb.pe_drain_mark(self.sPE)
                stg = zstg.acquire(pb, "scalar")
                sspl = self.sT[(zstg.i - 1) % 2]
                pb.wait("scalar", self.sPE, t_mm)
                t_ev = pb.op("scalar", lambda e, bank=bank, stg=stg: e.activation(
                    out=stg[:, :], in_=bank[:, 0:256], func=AF.Silu), mark=self.sAC)
                self.ps_rel[id(bank)] = (self.sAC, t_ev)
                pb.wait("gpsimd", self.sAC, t_ev)
                t_sp = pb.dma("gpsimd", self.sz_spill[tt, :, g * 256:(g + 1) * 256], stg[:, :], sspl)
                zstg.release(sspl, t_sp)
            zring.release(self.sPE, self.sPE.n)
        pb.barrier(self.by_eng, dma_sems=self.all_dma)
        self.pop_scope()

    def emit_xn_spill(self):
        pb = self.pb
        for kc in range(KC):
            pb.dma("gpsimd",
                   self.xn_spill.ap().rearrange("t k p c -> k p t c")[kc],
                   self.xn[kc].ap()[:, 4:4 + SEG].rearrange("p (t c) -> p t c", c=128), self.sG)
        pb.barrier(self.by_eng, dma_sems=self.all_dma)

    def emit_ffn(self):
        pb = self.pb
        sb = self.push_scope()
        qring = Ring([sb(f"qf{i}", [128, H]) for i in range(2)])
        self.sb_ffnb = sb("ffnb", [1, H], BF16)
        pb.dma("gpsimd", self.sb_ffnb[:, :], self.ffn_b, self.sG)
        zpring = Ring([sb(f"zps{i}", [128, 128]) for i in range(2)])
        wtmp = sb("wtmp", [128, H], BF16)
        wTg = Ring([sb(f"wTg{i}", [128, KC * 512], BF16) for i in range(2)])
        xnt = Ring([sb(f"xnt{i}", [128, KC * 128], BF16) for i in range(2)])
        gstg = Ring([sb(f"gstg{i}", [128, 512], BF16) for i in range(2)])
        banks = [self.ps_a, self.ps_b]
        pcnt = 0
        t_wfree = None
        for ng in range(8):
            wt = wTg.acquire(pb, "scalar")
            for ncl in range(4):
                nchunk = ng * 4 + ncl
                qf = qring.acquire(pb, "gpsimd")
                sqd = self.sQ[(qring.i - 1) % 2]
                t_q = pb.dma("gpsimd", qf[:, :], self.ffn_q[nchunk * 128:(nchunk + 1) * 128, :], sqd)
                zs = zpring.acquire(pb, "sync")
                szd = self.sS[(zpring.i - 1) % 2]
                t_z = pb.dma("sync", zs[:, 0:64], self.ffn_zp[nchunk * 128:(nchunk + 1) * 128, :], szd)
                t_s = pb.dma("sync", zs[:, 64:128], self.ffn_sc[nchunk * 128:(nchunk + 1) * 128, :], szd)
                pb.wait("vector", sqd, t_q)
                pb.wait("vector", szd, t_s)
                if t_wfree is not None:
                    pb.wait("vector", self.sPE, t_wfree)
                t2 = None
                for gq in range(64):
                    t2 = pb.op("vector", lambda e, gq=gq, qf=qf, zs=zs: e.tensor_scalar(
                        out=wtmp[:, gq * 64:(gq + 1) * 64], in0=qf[:, gq * 64:(gq + 1) * 64],
                        scalar1=zs[:, gq:gq + 1], scalar2=zs[:, 64 + gq:65 + gq],
                        op0=ALU.subtract, op1=ALU.mult), mark=self.sDV if gq == 63 else None)
                qring.release(self.sDV, t2)
                zpring.release(self.sDV, t2)
                pb.wait("tensor", self.sDV, t2)
                for gi in range(4):
                    bank_x = self.ps_xt.acquire(pb, "tensor")
                    for j in range(8):
                        kc = gi * 8 + j
                        pb.op("tensor", lambda e, kc=kc, j=j, bank_x=bank_x: e.transpose(
                            out=bank_x[:, j * 128:(j + 1) * 128], in_=wtmp[:, kc * 128:(kc + 1) * 128],
                            identity=self.sb_c["c_ident"][:, :]))
                    t_tp = pb.pe_drain_mark(self.sPE)
                    pb.wait("scalar", self.sPE, t_tp)
                    t_ev = None
                    for j in range(8):
                        kc = gi * 8 + j
                        t_ev = pb.op("scalar", lambda e, kc=kc, j=j, ncl=ncl, bank_x=bank_x, wt=wt: e.copy(
                            out=wt[:, kc * 512 + ncl * 128:kc * 512 + (ncl + 1) * 128],
                            in_=bank_x[:, j * 128:(j + 1) * 128]), mark=self.sAC if j == 7 else None)
                    self.ps_xt.release(self.sAC, t_ev)
                t_wfree = self.sPE.n
            t_build = self.sAC.n
            for tt in range(NCH):
                xt = xnt.acquire(pb, "sync")
                t_x = pb.dma("sync", xt.ap().rearrange("p (k c) -> p k c", c=128),
                             self.xn_spill[tt].rearrange("k p c -> p k c"), self.sW)
                bank = banks[pcnt % 2]
                pcnt += 1
                rel = self.ps_rel.get(id(bank))
                if rel is not None:
                    pb.wait("tensor", *rel)
                pb.wait("tensor", self.sW, t_x)
                if tt == 0:
                    pb.wait("tensor", self.sAC, t_build)
                for kc in range(KC):
                    pb.op("tensor", lambda e, kc=kc, bank=bank, xt=xt, wt=wt: e.matmul(
                        out=bank[:, :], lhsT=xt[:, kc * 128:(kc + 1) * 128],
                        rhs=wt[:, kc * 512:(kc + 1) * 512], start=(kc == 0), stop=False))
                pb.op("tensor", lambda e, ng=ng, bank=bank: e.matmul(
                    out=bank[:, :], lhsT=self.sb_c["c_ones1"][:, :],
                    rhs=self.sb_ffnb[:, ng * 512:(ng + 1) * 512], start=False, stop=True))
                t_mm = pb.pe_drain_mark(self.sPE)
                xnt.release(self.sPE, t_mm)
                stg = gstg.acquire(pb, "scalar")
                sspl = self.sT[(gstg.i - 1) % 2]
                pb.wait("scalar", self.sPE, t_mm)
                t_ev = pb.op("scalar", lambda e, bank=bank, stg=stg: e.activation(
                    out=stg[:, :], in_=bank[:, :], func=AF.Silu), mark=self.sAC)
                self.ps_rel[id(bank)] = (self.sAC, t_ev)
                pb.wait("gpsimd", self.sAC, t_ev)
                t_sp = pb.dma("gpsimd", self.g_spill[tt, :, ng * 512:(ng + 1) * 512], stg[:, :], sspl)
                gstg.release(sspl, t_sp)
            wTg.release(self.sPE, self.sPE.n)
        pb.barrier(self.by_eng, dma_sems=self.all_dma)
        self.pop_scope()

    def emit_gate(self):
        pb = self.pb
        sbr = self.push_scope()
        self.sb_rms = sbr("rms", [128, NCH])
        sb = self.push_scope()
        yring = Ring([sb(f"yg{i}", [128, NH * 128], BF16) for i in range(2)])
        szring = Ring([sb(f"szg{i}", [128, DSSM], BF16) for i in range(2)])
        ustg = Ring([sb(f"ustg{i}", [128, 8 * 128], BF16) for i in range(2)])
        sb_utm8 = sb("utm8", [128, 8 * 128], BF16)
        sb_stats = sb("gstats", [128, 64])
        sb_gdump = sb("gdump", [128, 128])
        sb_v = sb("vws", [128, 4])
        t_tp = None
        for tt in range(NCH):
            yb = yring.acquire(pb, "sync")
            t_y = None
            for hh in range(0, NH, 16):
                t_y = pb.dma("sync", yb.ap()[:, hh * 128:(hh + 16) * 128].rearrange("p (h c) -> p h c", c=128),
                             self.y_spill[tt, hh:hh + 16].rearrange("h p c -> p h c"), self.sW)
            szb = szring.acquire(pb, "sync")
            t_sz = pb.dma("sync", szb[:, :], self.sz_spill[tt], self.sW)
            pb.wait("vector", self.sW, max(t_y, t_sz))
            for grp in range(8):
                if t_tp is not None:
                    pb.wait("vector", self.sPE, t_tp)
                t_u = None
                for j in range(8):
                    h = grp * 8 + j
                    t_u = pb.op("vector", lambda e, h=h, j=j, yb=yb, szb=szb: e.tensor_tensor(
                        out=sb_utm8[:, j * 128:(j + 1) * 128], in0=yb[:, h * 128:(h + 1) * 128],
                        in1=szb[:, h * 128:(h + 1) * 128], op=ALU.mult),
                        mark=self.sDV if j == 7 else None)
                pb.wait("scalar", self.sDV, t_u)
                t_v = None
                for j in range(8):
                    h = grp * 8 + j
                    t_v = pb.op("scalar", lambda e, h=h, j=j: e.activation(
                        out=sb_gdump[:, :], in_=sb_utm8[:, j * 128:(j + 1) * 128], func=AF.Square,
                        accum_out=sb_stats[:, h:h + 1]), mark=self.sAC if j == 7 else None)
                bank_x = self.ps_xt.acquire(pb, "tensor")
                pb.wait("tensor", self.sAC, t_v)
                for j in range(8):
                    pb.op("tensor", lambda e, j=j, bank_x=bank_x: e.transpose(
                        out=bank_x[:, j * 128:(j + 1) * 128], in_=sb_utm8[:, j * 128:(j + 1) * 128],
                        identity=self.sb_c["c_ident"][:, :]))
                t_tp = pb.pe_drain_mark(self.sPE)
                stg = ustg.acquire(pb, "scalar")
                pb.wait("scalar", self.sPE, t_tp)
                t_ev = None
                for j in range(8):
                    h = grp * 8 + j
                    t_ev = pb.op("scalar", lambda e, h=h, j=j, stg=stg, bank_x=bank_x: e.activation(
                        out=stg[:, j * 128:(j + 1) * 128], in_=bank_x[:, j * 128:(j + 1) * 128],
                        func=AF.Copy, scale=self.sb_nw[:, h:h + 1]), mark=self.sAC if j == 7 else None)
                self.ps_xt.release(self.sAC, t_ev)
                sspl = self.sT[(ustg.i - 1) % 2]
                pb.wait("gpsimd", self.sAC, t_ev)
                t_sp = pb.dma("gpsimd",
                              self.u_spill.ap()[grp * 8:(grp + 1) * 8, tt].rearrange("b p c -> p b c"),
                              stg.ap().rearrange("p (b c) -> p b c", c=128), sspl)
                ustg.release(sspl, t_sp)
            yring.release(self.sDV, self.sDV.n)
            szring.release(self.sDV, self.sDV.n)
            pb.wait("vector", self.sAC, self.sAC.n)
            t = pb.op("vector", lambda e: e.tensor_reduce(
                out=sb_v[:, 0:1], in_=sb_stats[:, :], axis=mybir.AxisListType.X, op=ALU.add), mark=self.sDV)
            pb.wait("vector", self.sDV, t)
            t = pb.op("vector", lambda e: e.tensor_scalar(
                out=sb_v[:, 1:2], in0=sb_v[:, 0:1], scalar1=self.sb_fc[:, 3:4],
                scalar2=self.sb_fc[:, 0:1], op0=ALU.mult, op1=ALU.add), mark=self.sDV)
            pb.wait("scalar", self.sDV, t)
            t = pb.op("scalar", lambda e: e.activation(
                out=sb_v[:, 2:3], in_=sb_v[:, 1:2], func=AF.Ln), mark=self.sAC)
            pb.wait("scalar", self.sAC, t)
            t = pb.op("scalar", lambda e, tt=tt: e.activation(
                out=self.sb_rms[:, tt:tt + 1], in_=sb_v[:, 2:3], func=AF.Exp, scale=-0.5), mark=self.sAC)
            pb.wait("vector", self.sAC, t)
        pb.barrier(self.by_eng, dma_sems=self.all_dma)
        self.pop_scope()

    def emit_wout(self):
        pb = self.pb
        sb0 = self.push_scope()
        s_tm = [sb0(f"stm{i}", [128, H], BF16) for i in range(4)]
        banks = [self.ps_a, self.ps_b]
        pcnt = 0
        for qt in range(4):
            sb = self.push_scope()
            uq = sb(f"uq{qt}", [128, 64 * 512], BF16)
            wring = Ring([sb(f"wog{qt}_{i}", [128, 64 * 256], BF16) for i in range(2)])
            if qt > 0:
                pb.wait("gpsimd", self.sPE, self.sPE.n)
                pb.wait("gpsimd", self.sDV, self.sDV.n)
                pb.wait("gpsimd", self.sW, self.sW.n)
            for dc in range(NH):
                pb.dma("gpsimd",
                       uq.ap()[:, dc * 512:(dc + 1) * 512].rearrange("p (b c) -> p b c", c=128),
                       self.u_spill.ap()[dc, qt * 4:(qt + 1) * 4].rearrange("b p c -> p b c"), self.sA)
            t_u = self.sA.n
            for mg in range(16):
                wb = wring.acquire(pb, "gpsimd")
                swd = self.sQ[(wring.i - 1) % 2]
                t_w = pb.dma("gpsimd", wb.ap().rearrange("p (d m) -> p d m", m=256),
                             self.w_out_t[mg].rearrange("d p m -> p d m"), swd)
                for ti in range(4):
                    tt = qt * 4 + ti
                    bank = banks[pcnt % 2]
                    pcnt += 1
                    rel = self.ps_rel.get(id(bank))
                    if rel is not None:
                        pb.wait("tensor", *rel)
                    if ti == 0:
                        pb.wait("tensor", swd, t_w)
                        if mg == 0:
                            pb.wait("tensor", self.sA, t_u)
                    for dc in range(NH):
                        pb.op("tensor", lambda e, dc=dc, ti=ti, bank=bank, wb=wb, uq=uq: e.matmul(
                            out=bank[:, 0:256], lhsT=uq[:, dc * 512 + ti * 128:dc * 512 + (ti + 1) * 128],
                            rhs=wb[:, dc * 256:(dc + 1) * 256], start=(dc == 0), stop=(dc == NH - 1)))
                    t_mm = pb.pe_drain_mark(self.sPE)
                    pb.wait("vector", self.sPE, t_mm)
                    t_ev = pb.op("vector", lambda e, ti=ti, tt=tt, mg=mg, bank=bank: e.tensor_scalar(
                        out=s_tm[ti][:, mg * 256:(mg + 1) * 256], in0=bank[:, 0:256],
                        scalar1=self.sb_rms[:, tt:tt + 1], scalar2=None, op0=ALU.mult), mark=self.sDV)
                    self.ps_rel[id(bank)] = (self.sDV, t_ev)
                wring.release(self.sPE, self.sPE.n)
            self.pop_scope()
            sb = self.push_scope()
            gric = Ring([sb(f"gt{qt}_{i}", [128, H], BF16) for i in range(2)])
            xric = Ring([sb(f"xr{qt}_{i}", [128, H]) for i in range(2)])
            sb_t1 = sb(f"rest1_{qt}", [128, H])
            outb = Ring([sb(f"outb{qt}_{i}", [128, H]) for i in range(2)])
            pb.wait("sync", self.sPE, self.sPE.n)
            pb.wait("sync", self.sDV, self.sDV.n)
            for ti in range(4):
                tt = qt * 4 + ti
                gb = gric.acquire(pb, "sync")
                sgx = self.sS[ti % 2]
                t_g = pb.dma("sync", gb[:, :], self.g_spill[tt], sgx)
                xb = xric.acquire(pb, "sync")
                t_x = pb.dma("sync", xb[:, :], self.xin[128 + tt * 128:128 + (tt + 1) * 128, :], sgx)
                pb.wait("vector", sgx, max(t_g, t_x))
                t1 = pb.op("vector", lambda e, ti=ti, gb=gb, sb_t1=sb_t1: e.tensor_tensor(
                    out=sb_t1[:, :], in0=s_tm[ti][:, :], in1=gb[:, :], op=ALU.mult), mark=self.sDV)
                ob = outb.acquire(pb, "vector")
                pb.wait("vector", self.sDV, t1)
                t2 = pb.op("vector", lambda e, xb=xb, ob=ob, sb_t1=sb_t1: e.tensor_tensor(
                    out=ob[:, :], in0=xb[:, :], in1=sb_t1[:, :], op=ALU.add), mark=self.sDV)
                gric.release(self.sDV, t2)
                xric.release(self.sDV, t2)
                pb.wait("sync", self.sDV, t2)
                t_o = pb.dma("sync", self.out_h[tt * 128:(tt + 1) * 128, :], ob[:, :], self.sW)
                outb.release(self.sW, t_o)
            self.pop_scope()
        pb.barrier(self.by_eng, dma_sems=self.all_dma)
        self.pop_scope()
        self.pop_scope()

    def finalize(self):
        pb = self.pb
        for name, s in self.sems.items():
            pb.wait("sync", s, s.n)
        while self.scopes:
            self.pop_scope()
        block = self.ctx.enter_context(self.nc.Block())
        pb.emit(block)
        self.ctx.close()
        return self.nc, self.dbg


def build_program(cfg):
    b = Build(cfg)
    b.emit_init()
    if cfg.get("stop_after") == "init":
        return b.finalize()
    if cfg.get("pred_inputs"):
        b.emit_prefix_combine()
    b.alloc_xn()
    b.emit_phase_a()
    if cfg.get("stop_after") == "A":
        if "xn" in cfg.get("debug", ()):
            od = b.dbg_out("xn", [KC, 128, XW], F32)
            for kc in range(KC):
                b.pb.dma("gpsimd", od[kc, :, :], b.xn[kc][:, :], b.sG)
        if cfg.get("dbg_s1"):
            od = b.dbg_out("s1", [128, 17 * 8], F32)
            b.pb.dma("gpsimd", od, b.sb_s1dbg[:, :], b.sG)
        return b.finalize()
    b.alloc_phase_b()
    if cfg.get("state_out"):
        b.pb.op("vector", lambda e: e.memset(b.sb_logd[:, :], 0.0))
    b.emit_b0()
    b.emit_b1()
    if cfg.get("stop_after") == "B1":
        b.pb.barrier(b.by_eng, dma_sems=b.all_dma)
        dbgs = cfg.get("debug", ())
        if "xn" in dbgs:
            od = b.dbg_out("xn", [KC, 128, XW], F32)
            for kc in range(KC):
                b.pb.dma("gpsimd", od[kc, :, :], b.xn[kc][:, :], b.sG)
        if "smalls" in dbgs:
            for nm, t in (("dt", b.sb_dt), ("wddt", b.sb_wddt),
                          ("expc", b.sb_expc), ("et", b.sb_et)):
                od = b.dbg_out(nm, [128, NCH * 64], F32)
                b.pb.dma("gpsimd", od, t[:, :], b.sG)
        if "bc" in dbgs:
            for nm, t in (("Bfm", b.sb_Bfm), ("Cfm", b.sb_Cfm)):
                od = b.dbg_out(nm, [128, SEG], F32)
                b.pb.dma("gpsimd", od, t[:, 0:SEG], b.sG)
            od = b.dbg_out("G", [128, NCH * 128], F32)
            b.pb.dma("gpsimd", od, b.sb_G[:, :], b.sG)
        return b.finalize()
    b.emit_b2()
    if cfg.get("stop_after") == "B2":
        if "y" in cfg.get("debug", ()):
            od = b.dbg_out("y", [NCH, NH, 128, 128], F32)
            for tt in range(NCH):
                for hh in range(0, NH, 16):
                    b.pb.dma("gpsimd", od[tt, hh:hh + 16], b.y_spill[tt, hh:hh + 16], b.sG)
        return b.finalize()
    b.pop_scope()
    b.emit_b3()
    b.emit_xn_spill()
    if cfg.get("stop_after") == "B3":
        if "sz" in cfg.get("debug", ()):
            od = b.dbg_out("sz", [NCH, 128, DSSM], F32)
            for tt in range(NCH):
                b.pb.dma("gpsimd", od[tt], b.sz_spill[tt], b.sG)
        return b.finalize()
    b.pop_scope()
    b.emit_ffn()
    if cfg.get("stop_after") == "C":
        if "g" in cfg.get("debug", ()):
            od = b.dbg_out("g", [NCH, 128, H], F32)
            for tt in range(NCH):
                b.pb.dma("gpsimd", od[tt], b.g_spill[tt], b.sG)
        return b.finalize()
    b.emit_gate()
    if cfg.get("stop_after") == "F":
        od = b.dbg_out("u", [NH, NCH, 128, 128], F32)
        for hh in range(0, NH, 8):
            b.pb.dma("gpsimd", od[hh:hh + 8], b.u_spill.ap()[hh:hh + 8], b.sG)
        od2 = b.dbg_out("rms", [128, NCH], F32)
        b.pb.dma("gpsimd", od2, b.sb_rms[:, :], b.sG)
        return b.finalize()
    b.emit_wout()
    return b.finalize()


def _core_inputs(ins, core, cfg):
    b, seg = core // 4, core % 4
    x_full = np.asarray(ins["L0_attn_hidden_states"])[b]
    r0, r1 = seg * SEG, (seg + 1) * SEG
    ctx = np.zeros((3, H), np.float32) if seg == 0 else np.ascontiguousarray(x_full[r0 - 3:r0])
    W_in = np.asarray(ins["L0_attn_W_in"])
    m = {
        "xin": np.concatenate([np.zeros((125, H), np.float32), ctx, x_full[r0:r1]], 0).astype(np.float32),
        "w_x": np.ascontiguousarray(W_in[:, DSSM:2 * DSSM].reshape(KC, 128, NH, 128).transpose(2, 0, 1, 3)),
        "w_bc": np.ascontiguousarray(W_in[:, 2 * DSSM:2 * DSSM + 256].reshape(KC, 128, 2, 128).transpose(2, 0, 1, 3)),
        "w_dtt": np.ascontiguousarray(W_in[:, COL_DT:].reshape(KC, 128, NH)),
        "conv_w": np.ascontiguousarray(np.asarray(ins["L0_attn_conv_w"]).reshape(66, 128, 4).transpose(1, 0, 2)).reshape(128, 264),
        "dt_bias": np.asarray(ins["L0_attn_dt_bias"]).reshape(1, NH).astype(np.float32),
        "a_log": np.asarray(ins["L0_attn_A_log"]).reshape(1, NH).astype(np.float32),
        "d_par": np.asarray(ins["L0_attn_D_param"]).reshape(1, NH).astype(np.float32),
    }
    if not cfg.get("light_io"):
        m["nw_t"] = np.ascontiguousarray(np.asarray(ins["L0_attn_norm_weight"]).reshape(NH, 128).T).astype(np.float32)
        m["w_z"] = np.ascontiguousarray(W_in[:, COL_Z:DSSM].reshape(KC, 128, 16, 512).transpose(2, 0, 1, 3))
        m["w_out_t"] = np.ascontiguousarray(np.asarray(ins["L0_attn_W_out"]).reshape(NH, 128, 16, 256).transpose(2, 0, 1, 3))
        m["ffn_q"] = np.asarray(ins["L0_ffn_weight_q"]).astype(np.int32)
        m["ffn_sc"] = np.asarray(ins["L0_ffn_scale"]).astype(np.float32)
        m["ffn_zp"] = np.asarray(ins["L0_ffn_zp"]).astype(np.float32)
        m["ffn_b"] = np.asarray(ins["L0_ffn_bias"]).reshape(1, H).astype(np.float32)
    for k, v in CONSTS_NP.items():
        m[k] = v
    fcv = np.zeros((128, 8), np.float32)
    for i, v in enumerate(Build.FCONST_VALUES):
        fcv[:, i] = v
    m["fconsts_in"] = fcv
    return m


def kernel(**inputs):
    ncores = 8
    cfg1 = {"stop_after": "B2", "state_out": True, "r_init_input": True,
            "b2_skip": {"spill"}, "light_io": True}
    cfg2 = {"pred_inputs": True}
    nc1, _ = build_program(cfg1)
    maps1 = []
    for core in range(ncores):
        m = _core_inputs(inputs, core, cfg1)
        m["r_init"] = np.zeros((NH, 128, 128), np.float32)
        maps1.append(m)
    res1 = run_bass_kernel_spmd(nc1, maps1, core_ids=list(range(ncores)))
    s_loc = [np.asarray(res1.results[c]["s_local"], np.float32) for c in range(ncores)]
    logd = [np.asarray(res1.results[c]["log_d"], np.float32).reshape(NH) for c in range(ncores)]

    nc2, _ = build_program(cfg2)
    maps2 = []
    for core in range(ncores):
        b, seg = core // 4, core % 4
        m = _core_inputs(inputs, core, cfg2)
        sp = np.zeros((3, NH, 128, 128), np.float32)
        ld = np.zeros((3, NH), np.float32)
        for k in range(seg):
            src_core = b * 4 + seg - 1 - k
            sp[2 - k] = s_loc[src_core]
            ld[2 - k] = logd[src_core]
        m["s_pred"] = sp
        m["logd_pred"] = ld.reshape(1, 3 * NH)
        maps2.append(m)
    res2 = run_bass_kernel_spmd(nc2, maps2, core_ids=list(range(ncores)))
    out = np.empty((B, S, H), np.float32)
    for core in range(ncores):
        b, seg = core // 4, core % 4
        out[b, seg * SEG:(seg + 1) * SEG] = np.asarray(res2.results[core]["out"], np.float32)
    return out
